# version 01 (final)
# baseline (speedup 1.0000x reference)
import numpy as np
import ml_dtypes
import concourse.bass as bass
import concourse.mybir as mybir
from concourse.bass_utils import run_bass_kernel_spmd

F32 = mybir.dt.float32
BF16 = mybir.dt.bfloat16
U32 = mybir.dt.uint32
AF = mybir.ActivationFunctionType
ALU = mybir.AluOpType
AX = mybir.AxisListType
BFNP = ml_dtypes.bfloat16

T = 2048
D = 2048
NT = 16
OWN = 8
NQ = OWN * 128
N_EXP = 8
SCL = float(np.float32(1.0) / np.float32(np.sqrt(np.float32(128.0))))
NEG = -1.0e9
KB = 1024

TAPS = {}
PHASE_LIMIT = 99
PH2_GROUPS = (0, 8)
PH4_GROUPS = (0, 8)
MOE_EXPERTS = 8
MOE_SHARED = 4


def chunks(lo, hi, step=512):
    out = []
    c = lo
    while c < hi:
        out.append((c, min(step, hi - c)))
        c += step
    return out


def build_nc():
    nc = bass.Bass("TRN2", target_bir_lowering=False, debug=False)

    def din(name, shape, dt=F32):
        return nc.dram_tensor(name, list(shape), dt, kind="ExternalInput").ap()

    dd = {"x0": din("x0", (T, D))}
    for nm in ["wq1", "wk1", "wv1", "wo1", "wqi", "wki", "wvi", "woi"]:
        dd[nm] = din(nm, (128, 16, D))
    for nm in ["bq1", "bk1", "bv1", "bo1", "bqi", "bki", "bvi", "wln1", "wln2"]:
        dd[nm] = din(nm, (128, 16))
    dd["wrt"] = din("wrt", (128, 16, 8))
    dd["weg"] = din("weg", (128, 16, 1))
    dd["wg"] = din("wg", (8, 128, 11, 16, 128))
    dd["wu"] = din("wu", (8, 128, 11, 16, 128))
    dd["wd"] = din("wd", (8, 128, 11, D))
    dd["wsg"] = din("wsg", (128, 44, 16, 128))
    dd["wsu"] = din("wsu", (128, 44, 16, 128))
    dd["wsd"] = din("wsd", (128, 44, D))
    dd["identf"] = din("identf", (128, 128))
    dd["identb"] = din("identb", (128, 128), BF16)
    dd["ones2"] = din("ones2", (2, 128), BF16)
    dd["onesr"] = din("onesr", (1, 128), BF16)
    dd["ptmat"] = din("ptmat", (128, 128), BF16)
    dd["mask1q"] = din("mask1q", (128, 128), BF16)
    dd["mask1k"] = din("mask1k", (128, 128), BF16)
    dd["maskiq"] = din("maskiq", (128, 2, 128), BF16)
    dd["maskik"] = din("maskik", (128, 2, 128), BF16)
    dd["cosk"] = din("cosk", (128, T), BF16)
    dd["sink"] = din("sink", (128, T), BF16)
    dd["cosq"] = din("cosq", (128, NQ), BF16)
    dd["sinq"] = din("sinq", (128, NQ), BF16)
    dd["par"] = din("par", (1, 1), U32)

    out_d = nc.dram_tensor("out", [NQ, D], F32, kind="ExternalOutput").ap()
    tap_d = {}
    for ph in TAPS:
        for (tn, _k, _sl, tshape) in TAPS[ph]:
            tap_d[tn] = nc.dram_tensor("tap_" + tn, list(tshape), F32,
                                       kind="ExternalOutput").ap()

    from contextlib import ExitStack
    with ExitStack() as ctx:
        AR_B = 176 * KB
        arena = ctx.enter_context(
            nc.sbuf_tensor("arena", [128, AR_B], mybir.dt.uint8)).ap()

        def view(off, shape, dt):
            esz = mybir.dt.size(dt)
            n = int(np.prod(shape))
            assert off % 64 == 0 and off + n * esz <= AR_B, (off, shape, dt)
            ap = arena[:, off:off + n * esz].bitcast(dt)
            if len(shape) > 1:
                pat = ("p (" + " ".join(f"d{i}" for i in range(len(shape)))
                       + ") -> p " + " ".join(f"d{i}" for i in range(len(shape))))
                ap = ap.rearrange(pat, **{f"d{i}": s for i, s in enumerate(shape)})
            return ap

        oA, oB, oC, oD = 0, 64 * KB, 128 * KB, 160 * KB
        CS = oC + 16 * KB
        HXT = view(oA, (16, T), BF16)
        X1F = view(oA, (16, T), BF16)
        X1OWN = view(oA + 32 * KB, (8, T), BF16)
        CTXI = view(oA, (16, NQ), BF16)
        HMT = view(oA, (16, NQ), BF16)
        WGE = [view(oA + 32 * KB + i * 4 * KB, (16, 128), BF16) for i in range(2)]
        WUE = [view(oA + 40 * KB + i * 4 * KB, (16, 128), BF16) for i in range(2)]
        SILT = view(oA + 56 * KB, (1024,), BF16)
        WDEA = view(oA + 32 * KB, (6, D), BF16)
        CTX1 = view(oB, (16, T), BF16)
        H1T = view(oB, (16, T), BF16)
        ACC = view(oB, (8, T), F32)
        KTG = view(oC, (2, T), BF16)
        VBG = view(oC + 8 * KB, (16, 256), BF16)
        WT = [view(CS, (T,), BF16), view(CS + 4 * KB, (T,), BF16)]
        JUNK = view(CS + 8 * KB, (T,), BF16)
        BVREP = view(CS + 12 * KB, (T,), BF16)
        WREP = BVREP
        WOC = [view(oC + i * 16 * KB, (16, 512), BF16) for i in range(2)]
        WOCH = view(oC, (16, 1024), BF16)
        SGU = view(oC, (11, 1024), BF16)
        XT = view(oD, (T,), F32)
        XT2 = view(oD + 8 * KB, (T,), F32)
        WKB = view(oD, (16, 128), BF16)
        WQB = view(oD + 4 * KB, (16, 128), BF16)
        WVB = view(oD + 4 * KB, (16, 256), BF16)
        QTS = view(oD, (T,), BF16)
        CROWS = view(oD + 12 * KB, (T,), BF16)
        XSL = [view(oD + i * 2 * KB, (512,), F32) for i in range(2)]
        QTI = view(oD, (NQ,), BF16)
        CROWSI = view(oD + 2 * KB, (NQ,), BF16)
        COSQ = view(oD + 4 * KB, (NQ,), BF16)
        SINQ = view(oD + 6 * KB, (NQ,), BF16)
        COSK = view(oD + 8 * KB, (T,), BF16)
        SINK = view(oD + 12 * KB, (T,), BF16)
        ZT = view(oD + 8 * KB, (T,), F32)
        JUNK5 = view(oD, (T,), BF16)
        WKB4 = view(CS, (16, 128), BF16)
        WVB4 = view(CS + 4 * KB, (16, 256), BF16)

        def sbt(name, shape, dt):
            return ctx.enter_context(nc.sbuf_tensor(name, list(shape), dt)).ap()

        IDF = sbt("idf", (128, 128), F32)
        IDB = sbt("idb", (128, 128), BF16)
        ONES2 = sbt("on2", (2, 128), BF16)
        ONESR = sbt("onr", (1, 128), BF16)
        PT = sbt("ptb", (128, 128), BF16)
        M1Q = sbt("m1q", (128, 128), BF16)
        M1K = sbt("m1k", (128, 128), BF16)
        MIQ = sbt("miq", (128, 2, 128), BF16)
        MIK = sbt("mik", (128, 2, 128), BF16)
        BIAS = sbt("biasb", (128, 9, 16), F32)
        BQS = sbt("bqs", (128, 2, 16), F32)
        SM = sbt("smb", (128, 16), F32)
        SMN = sbt("smnb", (128, 16), F32)
        SMS = sbt("smsb", (128, 16), F32)
        S16C = sbt("s16c", (128, 16), F32)
        L16 = sbt("l16b", (128, 16), F32)
        CRF = sbt("crfb", (128, 16), F32)
        HIB = sbt("hibb", (128, 16), BF16)
        NHI = sbt("nhib", (128, 16), F32)
        CR2A = sbt("cr2a", (128, 16, 2), F32)
        LG8 = sbt("lg8b", (128, 8), F32)
        EXP8 = sbt("exp8b", (128, 8), F32)
        PR8 = sbt("pr8b", (128, 8), F32)
        T18 = sbt("t18b", (128, 8), F32)
        P28 = sbt("p28b", (128, 8), F32)
        BOHL = sbt("bohlb", (2, T), BF16)
        ROW1 = sbt("row1b", (1, 128), BF16)
        LOTMP = sbt("lotmpb", (1, T), BF16)
        RTMP = sbt("rtmpb", (128, 512), F32)
        PARB = sbt("parbb", (1, 1), U32)
        WRTB = sbt("wrtb", (128, 16, 8), BF16)
        WEGB = sbt("wegb", (128, 16, 1), BF16)
        COMB = sbt("combb", (128, 8, 8), F32)
        SIG = sbt("sigb", (128, 8), F32)
        STAT = sbt("statb", (128, 12), F32)
        BI = {nm: i for i, nm in enumerate(
            ["bq1", "bk1", "bv1", "bo1", "bqi", "bki", "bvi", "wln1", "wln2"])}

        PSA = ctx.enter_context(nc.psum_tensor("psA", [128, 2048], F32)).ap()
        PSB = ctx.enter_context(nc.psum_tensor("psB", [128, 2048], F32)).ap()

        class Sem:
            def __init__(self, name):
                self.h = ctx.enter_context(nc.semaphore(name))
                self.n = 0

        L, G, P, V, A, BAR = (Sem(x) for x in ["L", "G", "P", "V", "A", "BAR"])
        SCHED = {"sync": [], "tensor": [], "vector": [], "scalar": [], "gpsimd": []}
        EMIT_ON = [True]

        def gate(n):
            EMIT_ON[0] = (PHASE_LIMIT >= n)

        def st(eng, fn):
            if EMIT_ON[0]:
                SCHED[eng].append(fn)

        def barrier():
            BAR.n += 5
            tgt = BAR.n
            for eng in SCHED:
                def fn(e, tgt=tgt):
                    e.drain()
                    e.sem_inc(BAR.h, 1)
                    e.wait_ge(BAR.h, tgt)
                st(eng, fn)

        trk = {"A": {"a": 0, "v": 0}, "B": {"a": 0, "v": 0}}

        def ps_of(ps):
            return "A" if ps is PSA else "B"

        env = {}

        def emit_taps(phase):
            if phase not in TAPS:
                return
            barrier()
            for (tn, key, colsl, tshape) in TAPS[phase]:
                src = env[key]
                if colsl is not None:
                    src = src[:, colsl[0]:colsl[1]]
                nfree = int(np.prod(src.shape[1:]))
                dst = tap_d[tn]
                nsp = max(1, nfree // 4096)
                step = nfree // nsp
                for j in range(nsp):
                    G.n += 16
                    sfl = src if len(src.shape) == 2 else src.rearrange(
                        "p a b -> p (a b)")
                    st("gpsimd", lambda g, j=j, step=step, sfl=sfl, dst=dst:
                       g.dma_start(out=dst[:, j * step:(j + 1) * step],
                                   in_=sfl[:, j * step:(j + 1) * step])
                       .then_inc(G.h, 16))
            barrier()

        def mmset(t, specs):
            ins = None
            for (o, lh, rh, sa, so) in specs:
                ins = t.matmul(o, lh, rh, start=sa, stop=so)
            return ins

        def ph0(s):
            for src, dst in [(dd["identf"], IDF), (dd["identb"], IDB),
                             (dd["ones2"], ONES2), (dd["onesr"], ONESR),
                             (dd["ptmat"], PT), (dd["mask1q"], M1Q),
                             (dd["mask1k"], M1K), (dd["maskiq"], MIQ),
                             (dd["maskik"], MIK), (dd["par"], PARB)]:
                s.dma_start(out=dst, in_=src).then_inc(L.h, 16)
            for nm, i in BI.items():
                s.dma_start(out=BIAS[:, i, :], in_=dd[nm]).then_inc(L.h, 16)
        L.n += 16 * 19
        st("sync", ph0)
        lw0 = L.n
        V.n += 1
        st("vector", lambda v, lw0=lw0:
           (v.wait_ge(L.h, lw0),
            v.tensor_scalar_mul(BQS[:, 0, :], BIAS[:, BI["bq1"], :], SCL),
            v.tensor_scalar_mul(BQS[:, 1, :], BIAS[:, BI["bqi"], :], SCL)
            .then_inc(V.h, 1)))
        barrier()

        def emit_rep(bidx, dst_bf):
            for fc in range(16):
                vw = V.n
                P.n += 1
                pn1 = P.n
                st("tensor", lambda t, bidx=bidx, vw=vw, fc=fc:
                   (t.wait_ge(V.h, vw),
                    t.transpose(PSB[0:1, 1536:1664], BIAS[:, bidx, fc:fc + 1],
                                IDF).then_inc(P.h, 1)))
                V.n += 1
                vn1 = V.n
                st("vector", lambda v, pn1=pn1:
                   (v.wait_ge(P.h, pn1),
                    v.tensor_copy(ROW1[:, :], PSB[0:1, 1536:1664]).then_inc(V.h, 1)))
                P.n += 1
                pn2 = P.n
                st("tensor", lambda t, vn1=vn1:
                   (t.wait_ge(V.h, vn1),
                    t.matmul(PSB[:, 1024:1152], ONESR, ROW1, start=True,
                             stop=True).then_inc(P.h, 1)))
                V.n += 1
                st("vector", lambda v, pn2=pn2, fc=fc, dst_bf=dst_bf:
                   (v.wait_ge(P.h, pn2),
                    v.tensor_copy(dst_bf[:, fc * 128:(fc + 1) * 128],
                                  PSB[:, 1024:1152]).then_inc(V.h, 1)))
            trk["B"]["v"] = V.n

        def emit_bo_hilo():
            for fc in range(16):
                vw = V.n
                P.n += 1
                pn = P.n
                st("tensor", lambda t, fc=fc, vw=vw:
                   (t.wait_ge(V.h, vw),
                    t.transpose(PSB[0:1, 1536:1664],
                                BIAS[:, BI["bo1"], fc:fc + 1], IDF).then_inc(P.h, 1)))
                A.n += 1
                an_h = A.n
                st("scalar", lambda s, pn=pn, fc=fc:
                   (s.wait_ge(P.h, pn),
                    s.copy(BOHL[0:1, fc * 128:(fc + 1) * 128],
                           PSB[0:1, 1536:1664]).then_inc(A.h, 1)))
                V.n += 1
                st("vector", lambda v, an_h=an_h, fc=fc:
                   (v.wait_ge(A.h, an_h),
                    v.tensor_sub(LOTMP[0:1, fc * 128:(fc + 1) * 128],
                                 PSB[0:1, 1536:1664],
                                 BOHL[0:1, fc * 128:(fc + 1) * 128])
                    .then_inc(V.h, 1)))
            vlo = V.n
            G.n += 16
            st("gpsimd", lambda g, vlo=vlo:
               (g.wait_ge(V.h, vlo),
                g.dma_start(out=BOHL[1:2, :], in_=LOTMP[0:1, :]).then_inc(G.h, 16)))
            trk["B"]["v"] = V.n

        def emit_norm_tile(src_ap, dst_ap, kind, gain=None, eps=1e-5, waits=(), junk=None):
            vw0, aw0 = V.n, A.n
            A.n += 1
            jk = junk if junk is not None else JUNK
            st("scalar", lambda s, src=src_ap, waits=tuple(waits), vw0=vw0, jk=jk:
               ([s.wait_ge(w.h, wn) for (w, wn) in waits],
                s.wait_ge(V.h, vw0),
                s.activation(jk[:, :], src, AF.Square,
                             accum_out=STAT[:, 0:1]).then_inc(A.h, 1)))
            an = A.n
            V.n += 1
            if kind == "ln":
                st("vector", lambda v, src=src_ap, an=an:
                   (v.wait_ge(A.h, an),
                    v.reduce_sum(STAT[:, 1:2], src, axis=AX.X),
                    v.tensor_scalar(STAT[:, 3:4], STAT[:, 0:1], 1.0 / D, eps,
                                    ALU.mult, ALU.add).then_inc(V.h, 1)))
                vn = V.n
                A.n += 1
                st("scalar", lambda s, vn=vn:
                   (s.wait_ge(V.h, vn),
                    s.mul(STAT[:, 2:3], STAT[:, 1:2], -1.0 / D).then_inc(A.h, 1)))
                an = A.n
                V.n += 1
                st("vector", lambda v, an=an:
                   (v.wait_ge(A.h, an),
                    v.tensor_scalar(STAT[:, 4:5], STAT[:, 2:3], STAT[:, 2:3],
                                    -1.0, ALU.mult, ALU.mult).then_inc(V.h, 1)))
                vn = V.n
                A.n += 1
                st("scalar", lambda s, vn=vn:
                   (s.wait_ge(V.h, vn),
                    s.activation(STAT[:, 5:6], STAT[:, 3:4], AF.Identity,
                                 bias=STAT[:, 4:5]).then_inc(A.h, 1)))
                an = A.n
                V.n += 1
                st("vector", lambda v, an=an:
                   (v.wait_ge(A.h, an),
                    v.reciprocal(STAT[:, 6:7], STAT[:, 5:6]).then_inc(V.h, 1)))
                vn = V.n
                A.n += 1
                st("scalar", lambda s, vn=vn:
                   (s.wait_ge(V.h, vn),
                    s.activation(STAT[:, 7:8], STAT[:, 6:7], AF.Sqrt)
                    .then_inc(A.h, 1)))
                an = A.n
                V.n += 1
                st("vector", lambda v, src=src_ap, dst=dst_ap, an=an:
                   (v.wait_ge(A.h, an),
                    v.tensor_scalar(dst, src, STAT[:, 2:3], STAT[:, 7:8],
                                    ALU.add, ALU.mult).then_inc(V.h, 1)))
            else:
                st("vector", lambda v, an=an:
                   (v.wait_ge(A.h, an),
                    v.tensor_scalar(STAT[:, 3:4], STAT[:, 0:1], 1.0 / D, eps,
                                    ALU.mult, ALU.add).then_inc(V.h, 1)))
                vn = V.n
                A.n += 1
                st("scalar", lambda s, vn=vn:
                   (s.wait_ge(V.h, vn),
                    s.activation(STAT[:, 5:6], STAT[:, 3:4], AF.Sqrt)
                    .then_inc(A.h, 1)))
                an = A.n
                V.n += 1
                st("vector", lambda v, an=an:
                   (v.wait_ge(A.h, an),
                    v.reciprocal(STAT[:, 7:8], STAT[:, 5:6]).then_inc(V.h, 1)))
                vn = V.n
                A.n += 1
                st("scalar", lambda s, vn=vn:
                   (s.wait_ge(V.h, vn),
                    s.copy(STAT[:, 8:9], STAT[:, 7:8]).then_inc(A.h, 1)))
                an = A.n
                V.n += 1
                st("vector", lambda v, src=src_ap, dst=dst_ap, an=an, g=gain:
                   (v.wait_ge(A.h, an),
                    v.scalar_tensor_tensor(dst, src, STAT[:, 8:9], g,
                                           ALU.mult, ALU.mult).then_inc(V.h, 1)))

        def emit_transpose_tile(src_f32, dst_fn):
            slots = [0, 512, 1024, 1536]
            for fc in range(16):
                slot = slots[fc % 4]
                vw = V.n if fc < 4 else V.n - 3
                P.n += 1
                pn = P.n
                st("tensor", lambda t, src=src_f32, fc=fc, slot=slot, vw=vw:
                   (t.wait_ge(V.h, max(0, vw)),
                    t.transpose(PSB[:, slot:slot + 128],
                                src[:, fc * 128:(fc + 1) * 128], IDF)
                    .then_inc(P.h, 1)))
                V.n += 1
                st("vector", lambda v, pn=pn, fc=fc, slot=slot, dst_fn=dst_fn:
                   (v.wait_ge(P.h, pn),
                    v.tensor_copy(dst_fn(fc), PSB[:, slot:slot + 128])
                    .then_inc(V.h, 1)))
            trk["B"]["v"] = V.n

        gate(1)
        for ti in range(NT):
            buf = XT if ti % 2 == 0 else XT2
            pw = P.n
            lprev = L.n
            L.n += 16
            ln_ = L.n
            st("sync", lambda s, ti=ti, buf=buf, pw=pw, lprev=lprev:
               (s.wait_ge(P.h, pw), s.wait_ge(L.h, lprev),
                s.dma_start(out=buf[:, :],
                            in_=dd["x0"][ti * 128:(ti + 1) * 128, :])
                .then_inc(L.h, 16)))
            emit_norm_tile(buf[:, :], buf[:, :], "ln", eps=1e-5,
                           waits=[(L, ln_)])
            emit_transpose_tile(buf,
                                lambda fc, ti=ti: HXT[:, fc, ti * 128:(ti + 1) * 128])
        env.update(HXT=HXT)
        barrier()
        emit_taps("ph1")

        def load_w_chunk(src3, col0, ncol, dst):
            pw, vw, aw, gprev = P.n, V.n, A.n, G.n
            for q in range(4):
                G.n += 16
                st("gpsimd", lambda g, src3=src3, col0=col0, ncol=ncol,
                   dst=dst, q=q, pw=pw, vw=vw, aw=aw, gprev=gprev:
                   (g.wait_ge(P.h, pw), g.wait_ge(V.h, vw), g.wait_ge(A.h, aw),
                    g.wait_ge(G.h, gprev),
                    g.dma_start(out=dst[:, q * 4:(q + 1) * 4, 0:ncol],
                                in_=src3[:, q * 4:(q + 1) * 4,
                                         col0:col0 + ncol]).then_inc(G.h, 16)))

        def emit_kproj(wsrc, hsrc, kc, bias_col, dst2, n_tok=T, wbuf=None,
                       scale=1.0):
            gw = G.n
            for tc, (c0, cn) in enumerate(chunks(0, n_tok)):
                aw = trk["A"]["a"]
                P.n += 1
                pn = P.n
                st("tensor", lambda t, c0=c0, cn=cn, tc=tc, gw=gw, aw=aw,
                   hsrc=hsrc, wbuf=wbuf:
                   (t.wait_ge(G.h, gw),
                    t.wait_ge(A.h, aw),
                    mmset(t, [(PSA[:, (tc % 4) * 512:(tc % 4) * 512 + cn],
                               wbuf[:, fc, :], hsrc[:, fc, c0:c0 + cn],
                               fc == 0, fc == 15) for fc in range(16)])
                    .then_inc(P.h, 1)))
                A.n += 1
                st("scalar", lambda s, pn=pn, tc=tc, c0=c0, cn=cn, dst2=dst2,
                   kc=kc, bias_col=bias_col, scale=scale:
                   (s.wait_ge(P.h, pn),
                    s.activation(dst2[:, kc, c0:c0 + cn],
                                 PSA[:, (tc % 4) * 512:(tc % 4) * 512 + cn],
                                 AF.Identity, bias=bias_col, scale=scale)
                    .then_inc(A.h, 1)))
            trk["A"]["a"] = A.n

        def emit_rope(dst2kc, cos_t, sin_t, n_tok):
            for tc, (c0, cn) in enumerate(chunks(0, n_tok)):
                aw = trk["A"]["a"]
                vwb = V.n
                P.n += 1
                pn = P.n
                st("tensor", lambda t, c0=c0, cn=cn, tc=tc, dst2kc=dst2kc,
                   aw=aw, vwb=vwb:
                   (t.wait_ge(A.h, aw),
                    t.wait_ge(V.h, vwb),
                    t.matmul(PSB[:, (tc % 2) * 512:(tc % 2) * 512 + cn], PT,
                             dst2kc[:, c0:c0 + cn], start=True, stop=True)
                    .then_inc(P.h, 1)))
                V.n += 3
                st("vector", lambda v, pn=pn, c0=c0, cn=cn, tc=tc,
                   dst2kc=dst2kc, cos_t=cos_t, sin_t=sin_t:
                   (v.wait_ge(P.h, pn),
                    v.tensor_tensor(PSB[:, (tc % 2) * 512:(tc % 2) * 512 + cn],
                                    PSB[:, (tc % 2) * 512:(tc % 2) * 512 + cn],
                                    sin_t[:, c0:c0 + cn], ALU.mult).then_inc(V.h, 1),
                    v.tensor_tensor(RTMP[:, 0:cn], dst2kc[:, c0:c0 + cn],
                                    cos_t[:, c0:c0 + cn], ALU.mult).then_inc(V.h, 1),
                    v.tensor_tensor(dst2kc[:, c0:c0 + cn], RTMP[:, 0:cn],
                                    PSB[:, (tc % 2) * 512:(tc % 2) * 512 + cn],
                                    ALU.add).then_inc(V.h, 1)))
            trk["B"]["v"] = V.n

        def emit_vproj(wvb, hsrc, bvrep_sl):
            gw = G.n
            for tt in range(16):
                vwb = V.n if tt < 4 else V.n - 3
                P.n += 1
                pn = P.n
                st("tensor", lambda t, tt=tt, gw=gw, vwb=vwb, hsrc=hsrc, wvb=wvb:
                   (t.wait_ge(G.h, gw),
                    t.wait_ge(V.h, max(0, vwb)),
                    mmset(t, [(PSB[:, (tt % 4) * 512:(tt % 4) * 512 + 256],
                               hsrc[:, fc, tt * 128:(tt + 1) * 128],
                               wvb[:, fc, 0:256], fc == 0, fc == 15)
                              for fc in range(16)]).then_inc(P.h, 1)))
                V.n += 1
                st("vector", lambda v, pn=pn, tt=tt, bvrep_sl=bvrep_sl:
                   (v.wait_ge(P.h, pn),
                    v.tensor_tensor(VBG[:, tt, :],
                                    PSB[:, (tt % 4) * 512:(tt % 4) * 512 + 256],
                                    bvrep_sl, ALU.add).then_inc(V.h, 1)))
            trk["B"]["v"] = V.n

        def emit_attn_head(qT, kT1, hc, ctx_dst, q_tiles, ext_of, kt_list,
                           qcol0_of, maskq_of, maskk_of, crows_v, nq_all):
            last = {"A": (A.n, V.n), "B": (A.n, V.n)}
            for qi in q_tiles:
                ext = ext_of(qi)
                ps, key = (PSA, "A") if qi % 2 == 0 else (PSB, "B")
                aw, vw = last[key]
                P.n += 1
                pn = P.n
                mq = maskq_of(qi)
                mqcols = set(m0 // 512 for (m0, _mm) in mq)
                st("tensor", lambda t, qi=qi, ext=ext, ps=ps, aw=aw, vw=vw,
                   qT=qT, kT1=kT1, mq=mq, mqcols=mqcols:
                   (t.wait_ge(A.h, aw), t.wait_ge(V.h, vw),
                    mmset(t, [(ps[:, c0:c0 + cn], qT[:, qi * 128:(qi + 1) * 128],
                               kT1[:, c0:c0 + cn], True, c0 // 512 not in mqcols)
                              for (c0, cn) in chunks(0, ext)]
                          + [(ps[:, m0:m0 + 128], IDB, mm, False, True)
                             for (m0, mm) in mq]).then_inc(P.h, 1)))
                V.n += 1
                vn = V.n
                qslot = q_tiles.index(qi)
                st("vector", lambda v, pn=pn, ext=ext, ps=ps, qslot=qslot:
                   (v.wait_ge(P.h, pn),
                    v.reduce_max(SMN[:, qslot:qslot + 1], ps[:, 0:ext], axis=AX.X,
                                 negate=True).then_inc(V.h, 1)))
                A.n += 1
                an = A.n
                st("scalar", lambda s, vn=vn, ext=ext, ps=ps, qslot=qslot:
                   (s.wait_ge(V.h, vn),
                    s.activation(JUNK[:, 0:ext], ps[:, 0:ext], AF.Exp,
                                 bias=SMN[:, qslot:qslot + 1],
                                 accum_out=SMS[:, qslot:qslot + 1]).then_inc(A.h, 1)))
                last[key] = (A.n, V.n)
            nq_t = len(q_tiles)
            aw = A.n
            V.n += 1
            st("vector", lambda v, aw=aw, nq_t=nq_t:
               (v.wait_ge(A.h, aw),
                v.tensor_copy(S16C[:, 0:nq_t], SMS[:, 0:nq_t]).then_inc(V.h, 1)))
            vn = V.n
            A.n += 1
            st("scalar", lambda s, vn=vn, nq_t=nq_t:
               (s.wait_ge(V.h, vn),
                s.activation(L16[:, 0:nq_t], S16C[:, 0:nq_t], AF.Ln)
                .then_inc(A.h, 1)))
            an = A.n
            V.n += 1
            st("vector", lambda v, an=an, nq_t=nq_t:
               (v.wait_ge(A.h, an),
                v.tensor_sub(CRF[:, 0:nq_t], SMN[:, 0:nq_t], L16[:, 0:nq_t])
                .then_inc(V.h, 1)))
            vn = V.n
            A.n += 1
            st("scalar", lambda s, vn=vn, nq_t=nq_t:
               (s.wait_ge(V.h, vn),
                s.copy(HIB[:, 0:nq_t], CRF[:, 0:nq_t]).then_inc(A.h, 1)))
            an = A.n
            V.n += 1
            st("vector", lambda v, an=an, nq_t=nq_t:
               (v.wait_ge(A.h, an),
                v.tensor_copy(CR2A[:, 0:nq_t, 0], HIB[:, 0:nq_t]),
                v.tensor_scalar_mul(NHI[:, 0:nq_t], HIB[:, 0:nq_t], -1.0)
                .then_inc(V.h, 1)))
            vn = V.n
            A.n += 1
            st("scalar", lambda s, vn=vn:
               (s.wait_ge(V.h, vn),
                s.copy(STAT[:, 9:10], SMN[:, 0:1]).then_inc(A.h, 1)))
            an = A.n
            V.n += 1
            st("vector", lambda v, an=an, nq_t=nq_t:
               (v.wait_ge(A.h, an),
                v.tensor_add(CR2A[:, 0:nq_t, 1], CRF[:, 0:nq_t], NHI[:, 0:nq_t])
                .then_inc(V.h, 1)))
            for qslot, qi in enumerate(q_tiles):
                vw = V.n
                P.n += 1
                pn2 = P.n
                st("tensor", lambda t, an=an, vw=vw, qslot=qslot:
                   (t.wait_ge(A.h, an), t.wait_ge(V.h, vw),
                    t.transpose(PSA[0:2, 1792:1920], CR2A[:, qslot, :], IDF)
                    .then_inc(P.h, 1)))
                V.n += 1
                st("vector", lambda v, pn2=pn2, qi=qi, crows_v=crows_v:
                   (v.wait_ge(P.h, pn2),
                    v.tensor_copy(crows_v[0:2, qi * 128:(qi + 1) * 128],
                                  PSA[0:2, 1792:1920]).then_inc(V.h, 1)))
            ph1_end_v = V.n
            ph1_end_a = A.n
            slot_a = {0: 0, 1: 0}
            first_av = True
            for kt in kt_list:
                q0 = qcol0_of(kt)
                for gi, (g0, gn) in enumerate(chunks(q0, nq_all, 1024)):
                    slot = (gi % 2) * 1024
                    aw = max(slot_a[gi % 2], ph1_end_a)
                    P.n += 1
                    pn = P.n
                    mk = maskk_of(kt, g0, gn)
                    mkcol = (mk[0] - g0) // 512 if mk else -1
                    st("tensor", lambda t, kt=kt, g0=g0, gn=gn, slot=slot,
                       aw=aw, qT=qT, kT1=kT1, mk=mk, mkcol=mkcol,
                       crows_v=crows_v, ph1_end_v=ph1_end_v:
                       (t.wait_ge(A.h, aw), t.wait_ge(V.h, ph1_end_v),
                        mmset(t,
                              [(PSA[:, slot + (c0 - g0):slot + (c0 - g0) + cn],
                                kT1[:, kt * 128:(kt + 1) * 128],
                                qT[:, c0:c0 + cn], True, False)
                               for (c0, cn) in chunks(g0, g0 + gn)]
                              + [(PSA[:, slot + (c0 - g0):slot + (c0 - g0) + cn],
                                  ONES2, crows_v[0:2, c0:c0 + cn], False,
                                  (c0 - g0) // 512 != mkcol)
                                 for (c0, cn) in chunks(g0, g0 + gn)]
                              + ([(PSA[:, slot + (mk[0] - g0):
                                       slot + (mk[0] - g0) + 128],
                                   IDB, mk[1], False, True)] if mk else []))
                        .then_inc(P.h, 1)))
                    A.n += 1
                    an = A.n
                    slot_a[gi % 2] = an
                    st("scalar", lambda s, pn=pn, gi=gi, gn=gn, slot=slot:
                       (s.wait_ge(P.h, pn),
                        s.activation(WT[gi % 2][:, 0:gn], PSA[:, slot:slot + gn],
                                     AF.Exp).then_inc(A.h, 1)))
                    P.n += 1
                    vb_sl = VBG[:, kt, hc * 128:(hc + 1) * 128]
                    st("tensor", lambda t, an=an, kt=kt, g0=g0, gn=gn, gi=gi,
                       vb_sl=vb_sl, first_av=first_av, ph1_end_v=ph1_end_v:
                       (t.wait_ge(A.h, an),
                        (t.wait_ge(V.h, ph1_end_v) if first_av else None),
                        mmset(t, [(PSB[:, c0:c0 + cn], vb_sl,
                                   WT[gi % 2][:, c0 - g0:c0 - g0 + cn],
                                   kt == kt_list[0],
                                   kt == _klast(c0, kt_list, qcol0_of))
                                  for (c0, cn) in chunks(g0, g0 + gn)])
                        .then_inc(P.h, 1)))
                    first_av = False
            pn = P.n
            V.n += 1
            st("vector", lambda v, pn=pn, ctx_dst=ctx_dst, nq_all=nq_all:
               (v.wait_ge(P.h, pn),
                v.tensor_copy(ctx_dst, PSB[:, 0:nq_all]).then_inc(V.h, 1)))
            trk["A"]["a"] = A.n
            trk["A"]["v"] = V.n
            trk["B"]["a"] = A.n
            trk["B"]["v"] = V.n

        def _klast(c0, kt_list, qcol0_of):
            last = kt_list[0]
            for kt in kt_list:
                if qcol0_of(kt) <= c0 + 511:
                    last = kt
            return last

        gate(2)
        for g in range(*PH2_GROUPS):
            for kc in range(2):
                h = g * 2 + kc
                load_w_chunk(dd["wk1"], h * 128, 128, WKB)
                emit_kproj(None, HXT, kc, BIAS[:, BI["bk1"], h:h + 1], KTG,
                           wbuf=WKB)
            load_w_chunk(dd["wv1"], g * 256, 256, WVB)
            if g == PH2_GROUPS[0]:
                emit_rep(BI["bv1"], BVREP)
            emit_vproj(WVB, HXT, BVREP[:, g * 256:(g + 1) * 256])
            for hc in range(2):
                h = g * 2 + hc
                load_w_chunk(dd["wq1"], h * 128, 128, WQB)
                emit_kproj(None, HXT, 0, BQS[:, 0, h:h + 1],
                           QTS.rearrange("p (a b) -> p a b", a=1), wbuf=WQB,
                           scale=SCL)
                emit_attn_head(
                    qT=QTS, kT1=KTG[:, hc, :], hc=hc,
                    ctx_dst=CTX1[:, h, :],
                    q_tiles=list(range(16)),
                    ext_of=lambda qi: 128 * (qi + 1),
                    kt_list=list(range(16)),
                    qcol0_of=lambda kt: kt * 128,
                    maskq_of=lambda qi: [(qi * 128, M1Q)],
                    maskk_of=lambda kt, g0, gn:
                        ((kt * 128, M1K) if g0 <= kt * 128 < g0 + gn else None),
                    crows_v=CROWS, nq_all=T)
        env.update(CTX1=CTX1)
        barrier()
        emit_taps("ph2")

        gate(3)
        emit_bo_hilo()
        for of in range(4):
            gprev3 = G.n
            for q4 in range(4):
                G.n += 16
                st("gpsimd", lambda g, of=of, q4=q4, gprev3=gprev3:
                   (g.wait_ge(G.h, gprev3),
                    g.dma_start(out=WOC[of % 2][:, q4 * 4:(q4 + 1) * 4, :],
                                in_=dd["wo1"][:, q4 * 4:(q4 + 1) * 4,
                                              of * 512:(of + 1) * 512])
                    .then_inc(G.h, 16)))
            gw = G.n
            for qt in range(16):
                lprev = L.n
                L.n += 16
                ln_ = L.n
                vv = V.n
                st("sync", lambda s, qt=qt, of=of, vv=vv, lprev=lprev:
                   (s.wait_ge(V.h, max(0, vv - 1)),
                    s.wait_ge(L.h, lprev),
                    s.dma_start(out=XSL[qt % 2][:, :],
                                in_=dd["x0"][qt * 128:(qt + 1) * 128,
                                             of * 512:(of + 1) * 512])
                    .then_inc(L.h, 16)))
                vw = V.n - 3
                P.n += 1
                pn = P.n
                st("tensor", lambda t, qt=qt, of=of, gw=gw, vw=vw:
                   (t.wait_ge(G.h, gw), t.wait_ge(V.h, max(0, vw)),
                    mmset(t, [(PSA[:, (qt % 4) * 512:(qt % 4) * 512 + 512],
                               CTX1[:, fch, qt * 128:(qt + 1) * 128],
                               WOC[of % 2][:, fch, :], fch == 0, False)
                              for fch in range(16)]
                          + [(PSA[:, (qt % 4) * 512:(qt % 4) * 512 + 512],
                              ONES2, BOHL[0:2, of * 512:(of + 1) * 512],
                              False, True)]).then_inc(P.h, 1)))
                V.n += 1
                st("vector", lambda v, pn=pn, qt=qt, of=of, ln_=ln_:
                   (v.wait_ge(P.h, pn), v.wait_ge(L.h, ln_),
                    v.tensor_tensor(X1F[:, qt, of * 512:(of + 1) * 512],
                                    PSA[:, (qt % 4) * 512:(qt % 4) * 512 + 512],
                                    XSL[qt % 2][:, :], ALU.add).then_inc(V.h, 1)))
        env.update(X1F=X1F)
        barrier()
        emit_taps("ph3")

        gate(4)
        emit_rep(BI["wln1"], WREP)
        for ti in range(NT):
            buf = XT if ti % 2 == 0 else XT2
            emit_norm_tile(X1F[:, ti, :], buf[:, :], "ln", eps=1e-5)
            emit_norm_tile(buf[:, :], buf[:, :], "rms", gain=WREP, eps=1e-6)
            emit_transpose_tile(buf,
                                lambda fc, ti=ti: H1T[:, fc, ti * 128:(ti + 1) * 128])
        barrier()
        vend = V.n
        G.n += 16 * 8
        gend = G.n
        cbase = G.n - 16 * 8
        st("gpsimd", lambda g, vend=vend, cbase=cbase:
           (g.wait_ge(V.h, vend), _compact(g, cbase)))

        def _compact(g, base):
            r = g.alloc_register("parreg")
            g.reg_load(r, PARB[0:1, 0:1])
            with g.If_eq(r, 0):
                for j, i in enumerate(range(7, -1, -1)):
                    g.dma_start(out=X1OWN[:, i, :], in_=X1F[:, 2 * i, :]
                                ).then_inc(G.h, 16)
                    g.wait_ge(G.h, base + 16 * (j + 1))
            with g.Else():
                for j, i in enumerate(range(7, -1, -1)):
                    g.dma_start(out=X1OWN[:, i, :], in_=X1F[:, 2 * i + 1, :]
                                ).then_inc(G.h, 16)
                    g.wait_ge(G.h, base + 16 * (j + 1))
            g.end_ifs()
        env.update(H1T=H1T, X1OWN=X1OWN)
        barrier()
        emit_taps("ph35")

        gate(5)
        for src, dst in [(dd["cosk"], COSK), (dd["sink"], SINK),
                         (dd["cosq"], COSQ), (dd["sinq"], SINQ)]:
            L.n += 16
            st("sync", lambda s, src=src, dst=dst:
               s.dma_start(out=dst[:, :], in_=src).then_inc(L.h, 16))
        emit_rep(BI["bvi"], BVREP)
        lrope = L.n
        st("sync", lambda s, lrope=lrope: s.wait_ge(L.h, lrope))
        barrier()
        for g in range(*PH4_GROUPS):
            for kc in range(2):
                h = g * 2 + kc
                load_w_chunk(dd["wki"], h * 128, 128, WKB4)
                emit_kproj(None, H1T, kc, BIAS[:, BI["bki"], h:h + 1], KTG,
                           wbuf=WKB4)
                emit_rope(KTG[:, kc, :], COSK, SINK, T)
            load_w_chunk(dd["wvi"], g * 256, 256, WVB4)
            emit_vproj(WVB4, H1T, BVREP[:, g * 256:(g + 1) * 256])
            for hc in range(2):
                h = g * 2 + hc
                load_w_chunk(dd["wqi"], h * 128, 128, WKB4)
                emit_kproj(None, H1T, 0, BQS[:, 1, h:h + 1],
                           JUNK.rearrange("p (a b) -> p a b", a=1), wbuf=WKB4,
                           scale=SCL)
                aq = A.n
                G.n += 32
                gq = G.n
                st("gpsimd", lambda g2, aq=aq, h=h:
                   (g2.wait_ge(A.h, aq), _qgather(g2, h)))

                def _qgather(g2, h):
                    r = g2.alloc_register(f"parq{h}")
                    g2.reg_load(r, PARB[0:1, 0:1])
                    jv = JUNK.rearrange("p (a b c) -> p a b c", a=8, b=2)
                    qv = QTI.rearrange("p (a c) -> p a c", a=8)
                    with g2.If_eq(r, 0):
                        g2.dma_start(out=qv[:, 0:4, :], in_=jv[:, 0:4, 0, :]
                                     ).then_inc(G.h, 16)
                        g2.dma_start(out=qv[:, 4:8, :], in_=jv[:, 4:8, 0, :]
                                     ).then_inc(G.h, 16)
                    with g2.Else():
                        g2.dma_start(out=qv[:, 0:4, :], in_=jv[:, 0:4, 1, :]
                                     ).then_inc(G.h, 16)
                        g2.dma_start(out=qv[:, 4:8, :], in_=jv[:, 4:8, 1, :]
                                     ).then_inc(G.h, 16)
                    g2.end_ifs()
                st("tensor", lambda t, gq=gq: t.wait_ge(G.h, gq))
                st("vector", lambda v, gq=gq: v.wait_ge(G.h, gq))
                emit_rope(QTI, COSQ, SINQ, NQ)
                emit_attn_head(
                    qT=QTI, kT1=KTG[:, hc, :], hc=hc,
                    ctx_dst=CTXI[:, h, :],
                    q_tiles=list(range(8)),
                    ext_of=lambda qi: 256 * (qi + 1),
                    kt_list=list(range(16)),
                    qcol0_of=lambda kt: (kt // 2) * 128,
                    maskq_of=lambda qi: [(256 * qi, MIQ[:, 0, :]),
                                         (256 * qi + 128, MIQ[:, 1, :])],
                    maskk_of=lambda kt, g0, gn:
                        (((kt // 2) * 128, MIK[:, kt % 2, :])
                         if g0 <= (kt // 2) * 128 < g0 + gn else None),
                    crows_v=CROWSI, nq_all=NQ)
        env.update(CTXI=CTXI, CROWSI=CROWSI, KTG=KTG, QTI=QTI)
        barrier()
        emit_taps("ph4")

        gate(6)
        for half in range(2):
            gprev5 = G.n
            pw5 = P.n
            for hh in range(4):
                G.n += 16
                st("gpsimd", lambda g, half=half, hh=hh, gprev5=gprev5, pw5=pw5:
                   (g.wait_ge(G.h, gprev5), g.wait_ge(P.h, pw5),
                    g.dma_start(out=WOCH[:, hh * 4:(hh + 1) * 4, :],
                                in_=dd["woi"][:, hh * 4:(hh + 1) * 4,
                                              half * 1024:(half + 1) * 1024])
                    .then_inc(G.h, 16)))
            gw = G.n
            for i in range(8):
                emit_norm_tile(X1OWN[:, i, :], ZT[:, :], "ln", eps=1e-5, junk=JUNK5)
                zv = V.n
                for sub in range(2):
                    of0 = half * 1024 + sub * 512
                    vw = V.n - 2
                    P.n += 1
                    pn = P.n
                    st("tensor", lambda t, i=i, half=half, sub=sub, gw=gw,
                       vw=vw, zv=zv:
                       (t.wait_ge(G.h, gw), t.wait_ge(V.h, max(0, vw)),
                        t.wait_ge(V.h, zv),
                        mmset(t, [(PSA[:, (2 * i + sub) % 4 * 512:
                                       (2 * i + sub) % 4 * 512 + 512],
                                   CTXI[:, fch, i * 128:(i + 1) * 128],
                                   WOCH[:, fch, sub * 512:(sub + 1) * 512],
                                   fch == 0, fch == 15) for fch in range(16)])
                        .then_inc(P.h, 1)))
                    V.n += 1
                    st("vector", lambda v, pn=pn, i=i, of0=of0, sub=sub,
                       half=half:
                       (v.wait_ge(P.h, pn),
                        v.tensor_tensor(ACC[:, i, of0:of0 + 512],
                                        PSA[:, (2 * i + sub) % 4 * 512:
                                            (2 * i + sub) % 4 * 512 + 512],
                                        ZT[:, of0:of0 + 512], ALU.add)
                        .then_inc(V.h, 1)))
        env.update(ACC=ACC, CTXI2=CTXI, SGU=SGU)
        barrier()
        emit_taps("ph5")

        gate(7)
        emit_rep(BI["wln2"], WREP)
        for i in range(8):
            buf = XT if i % 2 == 0 else XT2
            emit_norm_tile(ACC[:, i, :], buf[:, :], "rms", gain=WREP, eps=1e-6)
            emit_transpose_tile(buf,
                                lambda fc, i=i: HMT[:, fc, i * 128:(i + 1) * 128])
        barrier()
        G.n += 32
        st("gpsimd", lambda g:
           (g.dma_start(out=WRTB, in_=dd["wrt"]).then_inc(G.h, 16),
            g.dma_start(out=WEGB, in_=dd["weg"]).then_inc(G.h, 16)))
        gw = G.n
        for i in range(8):
            vw = V.n
            aw = A.n
            P.n += 1
            pn = P.n
            st("tensor", lambda t, i=i, gw=gw, vw=vw, aw=aw:
               (t.wait_ge(G.h, gw), t.wait_ge(V.h, vw), t.wait_ge(A.h, aw),
                mmset(t, [(PSB[:, 0:8], HMT[:, fc, i * 128:(i + 1) * 128],
                           WRTB[:, fc, :], fc == 0, fc == 15)
                          for fc in range(16)]).then_inc(P.h, 1)))
            P.n += 1
            pn2 = P.n
            st("tensor", lambda t, i=i:
                mmset(t, [(PSB[:, 8:9], HMT[:, fc, i * 128:(i + 1) * 128],
                           WEGB[:, fc, :], fc == 0, fc == 15)
                          for fc in range(16)]).then_inc(P.h, 1))
            V.n += 1
            vn = V.n
            st("vector", lambda v, pn2=pn2:
               (v.wait_ge(P.h, pn2),
                v.tensor_copy(LG8, PSB[:, 0:8]),
                v.reduce_max(STAT[:, 0:1], PSB[:, 0:8], axis=AX.X, negate=True)
                .then_inc(V.h, 1)))
            A.n += 1
            an = A.n
            st("scalar", lambda s, vn=vn, i=i:
               (s.wait_ge(V.h, vn),
                s.activation(EXP8, LG8, AF.Exp,
                             bias=STAT[:, 0:1], accum_out=STAT[:, 1:2]),
                s.activation(SIG[:, i:i + 1], PSB[:, 8:9], AF.Sigmoid)
                .then_inc(A.h, 1)))
            V.n += 1
            vn = V.n
            st("vector", lambda v, an=an:
               (v.wait_ge(A.h, an),
                v.reciprocal(STAT[:, 2:3], STAT[:, 1:2]).then_inc(V.h, 1)))
            A.n += 1
            an = A.n
            st("scalar", lambda s, vn=vn:
               (s.wait_ge(V.h, vn),
                s.activation(PR8, EXP8, AF.Identity, scale=STAT[:, 2:3])
                .then_inc(A.h, 1)))
            V.n += 1
            vn = V.n
            st("vector", lambda v, an=an:
               (v.wait_ge(A.h, an),
                v.reduce_max(STAT[:, 3:4], PR8, axis=AX.X).then_inc(V.h, 1)))
            A.n += 1
            an = A.n
            st("scalar", lambda s, vn=vn:
               (s.wait_ge(V.h, vn),
                s.copy(STAT[:, 4:5], STAT[:, 3:4]).then_inc(A.h, 1)))
            V.n += 1
            vn = V.n
            st("vector", lambda v, an=an:
               (v.wait_ge(A.h, an),
                v.tensor_scalar(T18, PR8, STAT[:, 4:5], -2.0,
                                ALU.is_equal, ALU.mult).then_inc(V.h, 1)))
            A.n += 1
            an = A.n
            st("scalar", lambda s, vn=vn:
               (s.wait_ge(V.h, vn),
                s.copy(STAT[:, 5:6], STAT[:, 4:5]).then_inc(A.h, 1)))
            V.n += 1
            vn = V.n
            st("vector", lambda v, an=an:
               (v.wait_ge(A.h, an),
                v.tensor_add(P28, PR8, T18).then_inc(V.h, 1)))
            A.n += 1
            an = A.n
            st("scalar", lambda s, vn=vn:
               (s.wait_ge(V.h, vn),
                s.copy(STAT[:, 6:7], STAT[:, 5:6]).then_inc(A.h, 1)))
            V.n += 1
            vn = V.n
            st("vector", lambda v, an=an:
               (v.wait_ge(A.h, an),
                v.reduce_max(STAT[:, 7:8], P28, axis=AX.X).then_inc(V.h, 1)))
            A.n += 1
            an = A.n
            st("scalar", lambda s, vn=vn:
               (s.wait_ge(V.h, vn),
                s.copy(STAT[:, 8:9], STAT[:, 7:8]).then_inc(A.h, 1)))
            V.n += 1
            st("vector", lambda v, an=an, i=i:
               (v.wait_ge(A.h, an),
                v.scalar_tensor_tensor(COMB[:, i, :], PR8, STAT[:, 8:9], PR8,
                                       ALU.is_ge, ALU.mult).then_inc(V.h, 1)))
        for i in range(8):
            V.n += 1
            st("vector", lambda v, i=i:
               v.tensor_tensor(ACC[:, i, :], ACC[:, i, :], X1OWN[:, i, :],
                               ALU.add).then_inc(V.h, 1))
        env.update(HMT=HMT, COMB=COMB, SIG=SIG)
        barrier()
        emit_taps("ph55")

        gate(8)
        def emit_expert(wg_src, wu_src, wd_src, scale_ap_of, n_mi=11):
            for mi in range(n_mi):
                gprev = G.n
                pw6 = P.n - 1 if mi >= 2 else P.n
                G.n += 32
                gw = G.n
                st("gpsimd", lambda g, mi=mi, wg_src=wg_src, wu_src=wu_src,
                   gprev=gprev, pw6=pw6:
                   (g.wait_ge(G.h, gprev),
                    g.wait_ge(P.h, max(0, pw6)),
                    g.dma_start(out=WGE[mi % 2], in_=wg_src[:, mi, :, :])
                    .then_inc(G.h, 16),
                    g.dma_start(out=WUE[mi % 2], in_=wu_src[:, mi, :, :])
                    .then_inc(G.h, 16)))
                half_off = (mi % 2) * 1024
                aw = A.n if mi == 0 else A.n - 1
                vw = V.n if mi == 0 else V.n - 1
                P.n += 1
                pn = P.n
                st("tensor", lambda t, mi=mi, gw=gw, aw=aw, vw=vw, ho=half_off:
                   (t.wait_ge(G.h, gw), t.wait_ge(A.h, max(0, aw)),
                    t.wait_ge(V.h, max(0, vw)),
                    mmset(t, [(PSA[:, ho + tc * 512:ho + tc * 512 + 512],
                               WGE[mi % 2][:, fc, :],
                               HMT[:, fc, tc * 512:tc * 512 + 512],
                               fc == 0, fc == 15)
                              for tc in range(2) for fc in range(16)]
                          + [(PSB[:, ho + tc * 512:ho + tc * 512 + 512],
                              WUE[mi % 2][:, fc, :],
                              HMT[:, fc, tc * 512:tc * 512 + 512],
                              fc == 0, fc == 15)
                             for tc in range(2) for fc in range(16)])
                    .then_inc(P.h, 1)))
                A.n += 1
                an = A.n
                st("scalar", lambda s, pn=pn, ho=half_off:
                   (s.wait_ge(P.h, pn),
                    s.activation(SILT[:, :], PSA[:, ho:ho + 1024], AF.Silu)
                    .then_inc(A.h, 1)))
                V.n += 1
                st("vector", lambda v, an=an, mi=mi, ho=half_off:
                   (v.wait_ge(A.h, an),
                    v.tensor_tensor(SGU[:, mi, :], SILT[:, :],
                                    PSB[:, ho:ho + 1024], ALU.mult).then_inc(V.h, 1)))
            mih = [list(range(0, (n_mi + 1) // 2)),
                   list(range((n_mi + 1) // 2, n_mi))]
            for hi, mis in enumerate(mih):
                pw0, vw0, aw0 = P.n, V.n, A.n
                gprev6 = G.n
                for j, mi in enumerate(mis):
                    G.n += 16
                    st("gpsimd", lambda g, j=j, mi=mi, wd_src=wd_src,
                       pw0=pw0, vw0=vw0, aw0=aw0, gprev6=gprev6:
                       (g.wait_ge(P.h, pw0), g.wait_ge(V.h, vw0),
                        g.wait_ge(A.h, aw0), g.wait_ge(G.h, gprev6),
                        g.dma_start(out=WDEA[:, j, :], in_=wd_src[:, mi, :])
                        .then_inc(G.h, 16)))
                gw = G.n
                for i in range(8):
                    vw = V.n
                    P.n += 1
                    pn = P.n
                    st("tensor", lambda t, i=i, mis=mis, gw=gw, vw=vw:
                       (t.wait_ge(G.h, gw), t.wait_ge(V.h, max(0, vw)),
                        mmset(t, [(PSB[:, cc * 512:cc * 512 + 512],
                                   SGU[:, mi, i * 128:(i + 1) * 128],
                                   WDEA[:, j, cc * 512:cc * 512 + 512],
                                   j == 0, j == len(mis) - 1)
                                  for j, mi in enumerate(mis)
                                  for cc in range(4)]).then_inc(P.h, 1)))
                    V.n += 1
                    st("vector", lambda v, pn=pn, i=i, sc=scale_ap_of(i):
                       (v.wait_ge(P.h, pn),
                        v.scalar_tensor_tensor(ACC[:, i, :], PSB[:, 0:2048],
                                               sc, ACC[:, i, :], ALU.mult,
                                               ALU.add).then_inc(V.h, 1)))

        for e in range(MOE_EXPERTS):
            emit_expert(dd["wg"][e], dd["wu"][e], dd["wd"][e],
                        lambda i, e=e: COMB[:, i, e:e + 1])
        for q in range(MOE_SHARED):
            emit_expert(dd["wsg"][:, q * 11:(q + 1) * 11, :, :],
                        dd["wsu"][:, q * 11:(q + 1) * 11, :, :],
                        dd["wsd"][:, q * 11:(q + 1) * 11, :],
                        lambda i: SIG[:, i:i + 1])
        barrier()
        emit_taps("ph6")

        gate(9)
        vend = V.n
        L.n += 16 * 8
        lend = L.n
        def ph7(s, vend=vend, lend=lend):
            s.wait_ge(V.h, vend)
            for i in range(8):
                s.dma_start(out=out_d[i * 128:(i + 1) * 128, :],
                            in_=ACC[:, i, :]).then_inc(L.h, 16)
            s.wait_ge(L.h, lend)
        st("sync", ph7)
        barrier()

        block = ctx.enter_context(nc.Block())
        for eng in ["sync", "tensor", "vector", "scalar", "gpsimd"]:
            def runner(e, eng=eng):
                for fn in SCHED[eng]:
                    fn(e)
            getattr(block, eng)(runner)

    return nc


def _pfc(w):
    return np.ascontiguousarray(w.reshape(16, 128, w.shape[1]).transpose(1, 0, 2))


def _b16(b):
    return np.ascontiguousarray(b.reshape(16, 128).T)


def _tri_q():
    q = np.arange(128)[:, None]
    k = np.arange(128)[None, :]
    return np.where(k <= q, 0.0, NEG).astype(np.float32)


def _make_inputs(inputs, seq, p):
    x0 = np.ascontiguousarray(inputs["L0_attn_query"][seq])
    tri = _tri_q()
    m1q = tri.astype(BFNP)
    m1k = np.ascontiguousarray(tri.T).astype(BFNP)
    miq = np.zeros((128, 2, 128), np.float32)
    if p == 0:
        miq[:, 0, :] = tri
        miq[:, 1, :] = NEG
    else:
        miq[:, 0, :] = 0.0
        miq[:, 1, :] = tri
    mik = np.zeros((128, 2, 128), np.float32)
    mik[:, 0, :] = miq[:, 0, :].T
    mik[:, 1, :] = miq[:, 1, :].T
    inv = (1.0 / (1e6 ** (np.arange(0, 128, 2, np.float32) / 128.0))).astype(np.float32)
    tpos = np.arange(T, dtype=np.float32)
    ang = tpos[:, None] * inv[None, :]
    emb = np.concatenate([ang, ang], axis=1)
    cosk = np.cos(emb).T.astype(np.float32)
    sink = np.sin(emb).T.astype(np.float32)
    own_rows = (np.arange(NQ) // 128) * 256 + 128 * p + (np.arange(NQ) % 128)
    cosq = cosk[:, own_rows]
    sinq = sink[:, own_rows]
    ptm = np.zeros((128, 128), np.float32)
    for i in range(64):
        ptm[i + 64, i] = -1.0
        ptm[i, i + 64] = 1.0
    wg = inputs["L0_ffn_w_gate_e"]
    wu = inputs["L0_ffn_w_up_e"]
    wd = inputs["L0_ffn_w_down_e"]

    def moe_sw(w):
        return np.ascontiguousarray(
            w.reshape(8, 16, 128, 11, 128).transpose(0, 2, 3, 1, 4))

    def down_sw(w):
        return np.ascontiguousarray(
            w.reshape(8, 11, 128, 2048).transpose(0, 2, 1, 3))

    def sh_sw(w):
        return np.ascontiguousarray(
            w.reshape(16, 128, 44, 128).transpose(1, 2, 0, 3))

    def shd_sw(w):
        return np.ascontiguousarray(
            w.reshape(44, 128, 2048).transpose(1, 0, 2))

    im = {
        "x0": x0,
        "wq1": _pfc(inputs["L0_attn_w_q"]), "wk1": _pfc(inputs["L0_attn_w_k"]),
        "wv1": _pfc(inputs["L0_attn_w_v"]), "wo1": _pfc(inputs["L0_attn_w_o"]),
        "wqi": _pfc(inputs["L0_ffn_wq"]), "wki": _pfc(inputs["L0_ffn_wk"]),
        "wvi": _pfc(inputs["L0_ffn_wv"]), "woi": _pfc(inputs["L0_ffn_wo"]),
        "bq1": _b16(inputs["L0_attn_b_q"]), "bk1": _b16(inputs["L0_attn_b_k"]),
        "bv1": _b16(inputs["L0_attn_b_v"]), "bo1": _b16(inputs["L0_attn_b_o"]),
        "bqi": _b16(inputs["L0_ffn_bq"]), "bki": _b16(inputs["L0_ffn_bk"]),
        "bvi": _b16(inputs["L0_ffn_bv"]),
        "wln1": _b16(inputs["L0_ffn_w_ln1"]), "wln2": _b16(inputs["L0_ffn_w_ln2"]),
        "wrt": np.ascontiguousarray(
            inputs["L0_ffn_w_router"].reshape(16, 128, 8).transpose(1, 0, 2)),
        "weg": np.ascontiguousarray(
            inputs["L0_ffn_w_sh_eg"].reshape(16, 128, 1).transpose(1, 0, 2)),
        "wg": moe_sw(wg), "wu": moe_sw(wu), "wd": down_sw(wd),
        "wsg": sh_sw(inputs["L0_ffn_w_sh_gate"]),
        "wsu": sh_sw(inputs["L0_ffn_w_sh_up"]),
        "wsd": shd_sw(inputs["L0_ffn_w_sh_down"]),
        "identf": np.eye(128, dtype=np.float32),
        "identb": np.eye(128, dtype=np.float32).astype(BFNP),
        "ones2": np.ones((2, 128), BFNP),
        "onesr": np.ones((1, 128), BFNP),
        "ptmat": ptm.astype(BFNP),
        "mask1q": m1q, "mask1k": m1k,
        "maskiq": miq.astype(BFNP), "maskik": mik.astype(BFNP),
        "cosk": cosk.astype(BFNP), "sink": sink.astype(BFNP),
        "cosq": cosq.astype(BFNP), "sinq": sinq.astype(BFNP),
        "par": np.array([[p]], dtype=np.uint32),
    }
    return im


_NC_CACHE = {}


def kernel(**inputs):
    key = "prog"
    if key not in _NC_CACHE:
        _NC_CACHE[key] = build_nc()
    nc = _NC_CACHE[key]
    in_maps = []
    for c in range(8):
        in_maps.append(_make_inputs(inputs, c // 2, c % 2))
    res = run_bass_kernel_spmd(nc, in_maps, core_ids=list(range(8)))
    y = np.empty((4, T, D), np.float32)
    for c in range(8):
        seq, p = c // 2, c % 2
        o = res.results[c]["out"].reshape(8, 128, D)
        y[seq].reshape(16, 128, D)[p::2] = o
    return y
